# version 25
# baseline (speedup 1.0000x reference)
import contextlib
import os
import numpy as np
import ml_dtypes
import concourse.bass as bass
import concourse.mybir as mybir
from concourse.bass_utils import run_bass_kernel_spmd

F32 = mybir.dt.float32
BF16 = mybir.dt.bfloat16
I32 = mybir.dt.int32
AF = mybir.ActivationFunctionType
ALU = mybir.AluOpType
AX = mybir.AxisListType

NCORES = 8
B_CORE = 2
SEQ = 2048
DM = 1024
NH = 16
HD = 64
HID = 3072
NTB = 16
NQC = 4
HC = 24
EPS = 1e-5
TOK = B_CORE * SEQ
NTC = TOK // 512

ENGINES = ("tensor", "vector", "scalar", "sync", "gpsimd")


class Sem:
    def __init__(self, name):
        self.name = name
        self.n = 0
        self.h = None
        self.producer = None
        self.dma = False


class Ring:

    def __init__(self, k):
        self.k = k
        self.guard = [None] * k

    def acquire(self, sched, eng, i):
        g = self.guard[i % self.k]
        if g is not None:
            sched.wait(eng, g[0], g[1])

    def release(self, i, sem, val):
        self.guard[i % self.k] = (sem, val)


class Sched:
    def __init__(self):
        self.ops = {e: [] for e in ENGINES}
        self.sems = {}
        self.last_wait = {e: {} for e in ENGINES}

    def sem(self, name):
        return self.sems.setdefault(name, Sem(name))

    def wait(self, eng, sem, val):
        if val <= 0:
            return
        if sem.producer == eng and not sem.dma:
            return
        if self.last_wait[eng].get(sem.name, -1) >= val:
            return
        self.last_wait[eng][sem.name] = val
        self.ops[eng].append(lambda e, s=sem, v=val: e.wait_ge(s.h, v))

    def op(self, eng, f, inc=None, n=1):
        if inc is not None:
            if inc.producer is None:
                inc.producer = eng
            assert inc.producer == eng, (inc.name, inc.producer, eng)
            inc.n += n
            cnt = inc.n
            self.ops[eng].append(
                lambda e, f=f, s=inc, n=n: f(e).then_inc(s.h, n))
            return cnt
        self.ops[eng].append(f)
        return None

    def dma(self, eng, f, sem, n=16):
        assert sem.dma
        self.wait(eng, sem, sem.n)
        return self.op(eng, f, inc=sem, n=n)

    def drain(self, eng):
        self.ops[eng].append(lambda e: e.drain())

    def bump(self, eng, sem, n=1):
        if sem.producer is None:
            sem.producer = eng
        sem.n += n
        cnt = sem.n
        if eng in ("vector", "scalar"):
            self.ops[eng].append(lambda e, s=sem, n=n: e.drain().then_inc(s.h, n))
        else:
            self.ops[eng].append(lambda e, s=sem, n=n: e.nop().then_inc(s.h, n))
        return cnt


class NS:

    def __init__(self, **kw):
        self.__dict__.update(kw)


def _queue_weight_loads(sc, T):
    T.wloads = []
    for hc in range(HC):
        for nm, dst, srcd in (("g", T.wg_sb, T.wg_d), ("u", T.wu_sb, T.wu_d)):
            def ld(e, dst=dst, srcd=srcd, hc=hc):
                o = dst[:, 1024 * hc:1024 * (hc + 1)].rearrange(
                    "p (c j) -> p c j", j=128)
                i = srcd[hc].rearrange("(c p) j -> p c j", p=128)
                return e.dma_start(out=o, in_=i)
            T.wloads.append((nm, hc, ld))


def _emit_next_wload(sc, T):
    if not T.wloads:
        return
    nm, hc, ld = T.wloads.pop(0)
    sem = sc.sems["wA" if (len(T.wloads) % 2) else "wB"]
    T.w_cnt[(nm, hc)] = (sem, sc.dma("gpsimd", ld, sem))




def _emit_ln_stats(sc, T, xt, mean_ap=None, var_ap=None):
    if mean_ap is None:
        mean_ap = T.mv[:, 0:1]
    if var_ap is None:
        var_ap = T.mv[:, 1:2]
    sc.op("vector", lambda e, xt=xt: e.bn_stats(T.bn6[:, 0:6], xt[:, 0:512]))
    sc.op("vector", lambda e, xt=xt: e.bn_stats(T.bn6[:, 6:12], xt[:, 512:1024]))
    b3 = T.bn6.ap().rearrange("p (c k) -> p c k", k=3)
    means = b3[:, :, 1:2].squeeze(2)
    cvars = b3[:, :, 2:3].squeeze(2)
    L = T.lns
    sc.drain("vector")
    sc.op("vector", lambda e: e.reduce_sum(L[:, 0:1], means, AX.X))
    sc.op("vector", lambda e: e.reduce_sum(L[:, 1:2], cvars, AX.X))
    sc.op("vector", lambda e: e.tensor_mul(L[:, 4:8], means, means))
    sc.drain("vector")
    sc.op("vector", lambda e: e.reduce_sum(L[:, 2:3], L[:, 4:8], AX.X))
    sc.drain("vector")
    sc.op("vector", lambda e: e.tensor_scalar(mean_ap, L[:, 0:1], 0.25,
                                              None, ALU.mult))
    sc.drain("vector")
    sc.op("vector", lambda e: e.tensor_mul(L[:, 3:4], mean_ap, mean_ap))
    sc.drain("vector")
    sc.op("vector", lambda e: e.scalar_tensor_tensor(
        L[:, 3:4], L[:, 2:3], 0.25, L[:, 3:4], ALU.mult, ALU.subtract))
    sc.drain("vector")
    return sc.op("vector", lambda e: e.scalar_tensor_tensor(
        var_ap, L[:, 1:2], 1.0 / 1024.0, L[:, 3:4], ALU.mult, ALU.add),
        inc=sc.sems["stat"])


def _emit_ln_stage1(sc, T, P, b, j, xload):
    s = sc.sems
    gt = b * NTB + j
    t = j % 4
    slot = xload % 5
    P.xr.acquire(sc, "sync", xload)
    c_x = sc.dma("sync", lambda e, gt=gt, slot=slot: e.dma_start(
        out=P.xring[:, 1024 * slot:1024 * (slot + 1)],
        in_=T.x_d[128 * gt:128 * (gt + 1), :]), s["x"])
    xt = P.xring[:, 1024 * slot:1024 * (slot + 1)]
    sc.wait("vector", s["x"], c_x)
    c_st = _emit_ln_stats(sc, T, xt, T.mv4[:, t:t + 1], T.mv4[:, 4 + t:5 + t])
    return (j, slot, xload, c_st), xload + 1


def _emit_ln_stage2(sc, T, P, b, st1):
    s = sc.sems
    j, slot, xl, c_st = st1
    t = j % 4
    xt = P.xring[:, 1024 * slot:1024 * (slot + 1)]
    sc.wait("scalar", s["stat"], c_st)
    c_sd = sc.op("scalar", lambda e, t=t: e.activation(
        T.stdv4[:, t:t + 1], T.mv4[:, 4 + t:5 + t], AF.Sqrt, bias=T.eps1[:],
        scale=1.0), inc=s["std"])
    sc.wait("vector", s["std"], c_sd)
    sc.op("vector", lambda e, t=t: e.reciprocal(
        T.rstd4[:, t:t + 1], T.stdv4[:, t:t + 1]))
    sc.drain("vector")
    x1t3 = P.x1tm[:, 1040 * j:1040 * (j + 1)].rearrange("p (h s) -> p h s", s=65)
    c_ap = sc.op("vector", lambda e, x1t3=x1t3, xt=xt, t=t: e.tensor_scalar(
        x1t3[:, :, 0:64], xt.rearrange("p (h s) -> p h s", s=64),
        T.mv4[:, t:t + 1], T.rstd4[:, t:t + 1], ALU.subtract, ALU.mult),
        inc=s["app"])
    P.xr.release(xl, s["app"], c_ap)
    sc.op("vector", lambda e, x1t3=x1t3: e.memset(x1t3[:, :, 64:65], 1.0))
    sc.drain("vector")
    sc.op("vector", lambda e, x1t3=x1t3: e.reduce_sum(
        T.t16[:, 0:16], x1t3[:, :, 0:64], AX.X, apply_absolute_value=True))
    sc.op("vector", lambda e, x1t3=x1t3: e.reduce_max(
        T.t16[:, 16:32], x1t3[:, :, 0:64], AX.X, apply_absolute_value=True))
    sc.drain("vector")
    if j == 0:
        c_m = sc.op("vector", lambda e: e.tensor_copy(T.m12[:], T.t16[:]),
                    inc=s["gpm"])
    else:
        c_m = sc.op("vector", lambda e: e.tensor_max(T.m12[:], T.m12[:], T.t16[:]),
                    inc=s["gpm"])
    T.m12_cnt = c_m
    return c_ap


def _emit_ln_tp(sc, T, P, b, j, c_ap):
    s = sc.sems
    for c in range(8):
        hk = "A" if (j * 8 + c) % 2 == 0 else "B"
        P.ptr[hk].acquire(sc, "tensor", 0)
        sc.wait("tensor", s["app"], c_ap)
        def tp_e(e, j=j, c=c, hk=hk):
            src = P.x1tm[:, 1040 * j + 65 * (2 * c):1040 * j + 65 * (2 * c) + 64]
            return e.transpose(P.pt[hk][0:64, 0:128], src, T.idb[:])
        sc.op("tensor", tp_e)
        def tp_o(e, j=j, c=c, hk=hk):
            src = P.x1tm[:, 1040 * j + 65 * (2 * c + 1):
                         1040 * j + 65 * (2 * c + 1) + 64]
            return e.transpose(P.pt[hk][64:128, 0:128], src, T.idb[:],
                               tile_position=(0, 64))
        c_tp = sc.op("tensor", tp_o, inc=s["tp1"])
        sc.wait("vector", s["tp1"], c_tp)
        def ev(e, j=j, c=c, hk=hk):
            return e.tensor_copy(
                P.x1fm[:, 2048 * c + 128 * j: 2048 * c + 128 * (j + 1)],
                P.pt[hk][:, 0:128])
        c_ev = sc.op("vector", ev, inc=s["ev1"])
        P.ptr[hk].release(0, s["ev1"], c_ev)


def _emit_g_chain(sc, T, P, b, qc):
    s = sc.sems
    sc.drain("vector")
    c_mb = sc.op("vector", lambda e: e.tensor_copy(T.mbf[:], T.m12[:]), inc=s["gv"])
    P.ptr["A"].acquire(sc, "tensor", 0)
    sc.wait("tensor", s["gv"], c_mb)
    sc.op("tensor", lambda e: e.transpose(
        P.ptA[0:16, 0:128], T.mbf[:, 0:16], T.idb[:]))
    c_g1 = sc.op("tensor", lambda e: e.transpose(
        P.ptA[0:16, 128:256], T.mbf[:, 16:32], T.idb[:]), inc=s["gt"])
    sc.wait("vector", s["gt"], c_g1)
    sc.op("vector", lambda e: e.reduce_max(T.l1m[:], P.ptA[0:16, 0:128], AX.X))
    sc.op("vector", lambda e: e.reduce_max(T.abm[:], P.ptA[0:16, 128:256], AX.X))
    sc.drain("vector")
    sc.op("vector", lambda e: e.tensor_mul(T.g16[:], T.l1m[:], T.abm[:]))
    sc.drain("vector")
    c_g2 = sc.op("vector", lambda e: e.tensor_scalar(
        T.ng16b[:], T.g16[:], -0.125, None, ALU.mult), inc=s["gv"])
    sc.wait("tensor", s["gv"], c_g2)
    c_g3 = sc.op("tensor", lambda e: e.transpose(
        P.ptA[0:1, 256:272], T.ng16b[:], T.idb[0:16, 0:16]), inc=s["gt"])
    sc.wait("scalar", s["gt"], c_g3)
    c_g4 = sc.op("scalar", lambda e: e.copy(
        T.ngrow[:], P.ptA[0:1, 256:272]), inc=s["gs"])
    P.ptr["A"].release(0, s["gs"], c_g4)
    P.por["A"].acquire(sc, "tensor", 0)
    sc.wait("tensor", s["gs"], c_g4)
    c_g5 = sc.op("tensor", lambda e: e.matmul(
        P.poA[0:128, 0:16], T.ones_row[:], T.ngrow[:],
        start=True, stop=True), inc=s["gt"])
    sc.wait("vector", s["gt"], c_g5)
    col = 16 * ((b * NQC + qc) % 2)
    sc.wait("vector", s["exp"], s["exp"].n)
    sc.op("vector", lambda e, col=col: e.tensor_copy(
        T.negG[:, col:col + 16], P.poA[:, 0:16]))
    sc.drain("vector")
    g3 = T.negG[:, col:col + 16].rearrange("p (h two) -> p h two", two=2)
    pc = 8 * ((b * NQC + qc) % 2)
    c_g6 = sc.op("vector", lambda e, g3=g3, pc=pc: e.tensor_tensor(
        T.negGP[:, pc:pc + 8], g3[:, :, 0:1].squeeze(2),
        g3[:, :, 1:2].squeeze(2), ALU.min), inc=s["g"])
    P.por["A"].release(0, s["g"], c_g6)


def _emit_attn_unit(sc, T, P, b, qc, pair, ev1_gate):
    s = sc.sems
    pcol = 8 * ((b * NQC + qc) % 2) + pair
    c_gq = s["g"].n
    kjs = list(range(4 * qc, 4 * qc + 4)) + list(range(4 * qc))
    K = len(kjs)
    exp_at, msk_at, rel_of = {}, {}, {}
    pv_stop = {}

    def emit_pv(i, kj):
        rel = rel_of[i]
        ql = 512 - rel
        for hk, hid, hoff in (("A", 2 * pair, 0), ("B", 2 * pair + 1, 512)):
            if kj >= 4 * qc:
                sc.wait("tensor", s["msk"], msk_at[(i, hk)])
            else:
                sc.wait("tensor", s["exp"], exp_at[i])
            if i == 0:
                P.por[hk].acquire(sc, "tensor", 0)
            def pv(e, hk=hk, hid=hid, kj=kj, i=i, rel=rel, ql=ql, hoff=hoff):
                va = P.x1tm[:, 1040 * kj + 65 * hid:1040 * kj + 65 * hid + 128]
                return e.matmul(
                    P.po[hk][0:128, rel:rel + ql], va,
                    P.pbuf[:, 1024 * (i % 2) + hoff + rel:
                           1024 * (i % 2) + hoff + 512],
                    start=(i == 0), stop=(i == K - 1))
            cnt = sc.op("tensor", pv, inc=s["pvs"])
            if hk == "B":
                P.pbr.release(i, s["pvs"], cnt)
            if i == K - 1:
                pv_stop[hk] = cnt

    for i, kj in enumerate(kjs):
        sb = i % 2
        rel_of[i] = max(0, 128 * kj - 512 * qc)
        sc.wait("tensor", s["ev1"], ev1_gate)
        for hk, r0, tpos, hoff in (("A", 0, (0, 0), 0),
                                   ("B", 64, (64, 0), 512)):
            P.psr[sb].acquire(sc, "tensor", 0)
            def smm(e, r0=r0, tpos=tpos, pair=pair, kj=kj, qc=qc, sb=sb,
                    hoff=hoff):
                lhs = P.x1fm[r0:r0 + 64,
                             2048 * pair + 128 * kj:2048 * pair + 128 * (kj + 1)]
                rhs = P.x1fm[r0:r0 + 64,
                             2048 * pair + 512 * qc:2048 * pair + 512 * (qc + 1)]
                return e.matmul(P.psP[sb][:, hoff:hoff + 512], lhs, rhs,
                                start=True, stop=True, tile_position=tpos)
            c_s = sc.op("tensor", smm, inc=(s["scr"] if hk == "B" else None))
        sc.wait("scalar", s["scr"], c_s)
        sc.wait("scalar", s["g"], c_gq)
        P.pbr.acquire(sc, "scalar", i)
        rel = rel_of[i]
        if rel == 0:
            def ex(e, sb=sb, pcol=pcol):
                return e.activation(
                    P.pbuf[:, 1024 * sb:1024 * sb + 1024],
                    P.psP[sb][:, 0:1024], AF.Exp,
                    bias=T.negGP[:, pcol:pcol + 1], scale=0.125)
            c_e = sc.op("scalar", ex, inc=s["exp"])
        else:
            for hoff in (0, 512):
                def ex(e, sb=sb, pcol=pcol, hoff=hoff, rel=rel):
                    return e.activation(
                        P.pbuf[:, 1024 * sb + hoff + rel:1024 * sb + hoff + 512],
                        P.psP[sb][:, hoff + rel:hoff + 512], AF.Exp,
                        bias=T.negGP[:, pcol:pcol + 1], scale=0.125)
                c_e = sc.op("scalar", ex, inc=s["exp"])
        exp_at[i] = c_e
        P.psr[sb].release(0, s["exp"], c_e)
        if kj >= 4 * qc:
            sc.wait("gpsimd", s["exp"], c_e)
            for hk, hoff in (("A", 0), ("B", 512)):
                def mk(e, sb=sb, hoff=hoff, rel=rel):
                    o = 1024 * sb + hoff + rel
                    return e.tensor_mul(P.pbuf[:, o:o + 128],
                                        P.pbuf[:, o:o + 128], T.tri[:])
                msk_at[(i, hk)] = sc.op("gpsimd", mk, inc=s["msk"])
        if i >= 1:
            emit_pv(i - 1, kjs[i - 1])
    emit_pv(K - 1, kjs[K - 1])

    c_oe = {}
    for hk in ("A", "B"):
        sc.wait("vector", s["pvs"], pv_stop[hk])
        c_oe[hk] = sc.op("vector", lambda e, hk=hk: e.tensor_copy(
            P.ot[hk][0:65, :], P.po[hk][0:65, :]), inc=s["oev"])
        P.por[hk].release(0, s["oev"], c_oe[hk])
    for hk, hid in (("A", 2 * pair), ("B", 2 * pair + 1)):
        sc.wait("tensor", s["oev"], c_oe[hk])
        P.ptr[hk].acquire(sc, "tensor", 0)
        c_ot = None
        for c in range(4):
            def otp(e, hk=hk, c=c):
                return e.transpose(
                    P.pt[hk][:, 66 * c:66 * c + 65],
                    P.ot[hk][0:65, 128 * c:128 * (c + 1)], T.idb[0:65, 0:65])
            c_ot = sc.op("tensor", otp, inc=s["ot"])
        sc.wait("vector", s["ot"], c_ot)
        pt3 = P.pt[hk][:, 0:264].rearrange("p (c s) -> p c s", s=66)
        roff = 0 if hk == "A" else 4
        sc.op("vector", lambda e, pt3=pt3, roff=roff: e.reciprocal(
            T.rl[:, roff:roff + 4], pt3[:, 0:4, 64:65].squeeze(2)))
        sc.drain("vector")
        c_nm = None
        for c in range(4):
            P.stg.acquire(sc, "vector", c)
            def nrm(e, pt3=pt3, c=c, hid=hid, roff=roff):
                return e.tensor_scalar(
                    P.stage[:, 1024 * c + 64 * hid:1024 * c + 64 * (hid + 1)],
                    pt3[:, c:c + 1, 0:64].squeeze(1),
                    T.rl[:, roff + c:roff + c + 1], None, ALU.mult)
            c_nm = sc.op("vector", nrm, inc=s["nrm"])
        P.ptr[hk].release(0, s["nrm"], c_nm)


def _emit_res_loads(sc, T, P, b, qc, xload):
    s = sc.sems
    info = []
    for cc in range(4):
        gt = b * NTB + qc * 4 + cc
        slot = xload % 5
        P.xr.acquire(sc, "sync", xload)
        c_x = sc.dma("sync", lambda e, gt=gt, slot=slot: e.dma_start(
            out=P.xring[:, 1024 * slot:1024 * (slot + 1)],
            in_=T.x_d[128 * gt:128 * (gt + 1), :]), s["x"])
        info.append((slot, c_x, xload))
        xload += 1
    return info, xload


def _emit_residual_qc(sc, T, P, b, qc, res_info):
    s = sc.sems
    sc.wait("gpsimd", s["nrm"], s["nrm"].n)
    for cc in range(4):
        gt = b * NTB + qc * 4 + cc
        slot, c_x, xl = res_info[cc]
        sc.wait("gpsimd", s["x"], c_x)
        def radd(e, slot=slot, cc=cc):
            st = P.stage[:, 1024 * cc:1024 * (cc + 1)]
            return e.tensor_add(st, st,
                                P.xring[:, 1024 * slot:1024 * (slot + 1)])
        c_r = sc.op("gpsimd", radd, inc=s["x2r"])
        P.xr.release(xl, s["x2r"], c_r)
        sc.wait("sync", s["x2r"], c_r)
        c_so = sc.dma("sync", lambda e, gt=gt, cc=cc: e.dma_start(
            out=T.x2_d[128 * gt:128 * (gt + 1), :],
            in_=P.stage[:, 1024 * cc:1024 * (cc + 1)]), s["x2o"])
        P.stg.release(cc, s["x2o"], c_so)


def _emit_phase1(sc, T, P):
    s = sc.sems
    xload = 0
    ot_total = 0
    sc.op("vector", lambda e: e.memset(P.x1tm[:, NTB * NH * 65:], 0.0))
    for b in range(B_CORE):
        ev1_at_tile = {}
        st1s = []
        for j in range(4):
            st1, xload = _emit_ln_stage1(sc, T, P, b, j, xload)
            st1s.append(st1)
        if b > 0:
            sc.wait("vector", s["ot"], ot_total)
        for st1 in st1s:
            c_ap = _emit_ln_stage2(sc, T, P, b, st1)
            _emit_ln_tp(sc, T, P, b, st1[0], c_ap)
            ev1_at_tile[st1[0]] = s["ev1"].n
        _emit_g_chain(sc, T, P, b, 0)
        for qc in range(NQC):
            pending_s2 = None
            pending_tp = None
            for pair in range(8):
                if pair == 6:
                    res_info, xload = _emit_res_loads(sc, T, P, b, qc, xload)
                _emit_attn_unit(sc, T, P, b, qc, pair, ev1_at_tile[4 * qc + 3])
                if pending_tp is not None:
                    j, c_ap = pending_tp
                    _emit_ln_tp(sc, T, P, b, j, c_ap)
                    ev1_at_tile[j] = s["ev1"].n
                    pending_tp = None
                if pending_s2 is not None:
                    c_ap = _emit_ln_stage2(sc, T, P, b, pending_s2)
                    pending_tp = (pending_s2[0], c_ap)
                    if pending_s2[0] % 4 == 3:
                        _emit_g_chain(sc, T, P, b, qc + 1)
                    pending_s2 = None
                _emit_next_wload(sc, T)
                if pair % 2 == 0 and qc < NQC - 1:
                    j = 4 * (qc + 1) + pair // 2
                    pending_s2, xload = _emit_ln_stage1(sc, T, P, b, j, xload)
            if pending_tp is not None:
                j, c_ap = pending_tp
                _emit_ln_tp(sc, T, P, b, j, c_ap)
                ev1_at_tile[j] = s["ev1"].n
            _emit_residual_qc(sc, T, P, b, qc, res_info)
        if b == 0:
            ot_total = s["ot"].n
    while getattr(T, "wloads", None):
        _emit_next_wload(sc, T)
    for e in ENGINES:
        if e == "tensor":
            continue
        sc.bump(e, s["bar"])


def _emit_ffn_ln_a(sc, T, Q, t, tj, slots):
    s = sc.sems
    gt = 4 * t + tj
    slot = gt % 3
    slots[tj] = slot
    Q.x2r_ring.acquire(sc, "sync", gt)
    if s["x2o"].n:
        sc.wait("sync", s["x2o"], 16 * (gt + 1))
    c_xi = sc.dma("sync", lambda e, gt=gt, slot=slot: e.dma_start(
        out=Q.x2ring[:, 1024 * slot:1024 * (slot + 1)],
        in_=T.x2_d[128 * gt:128 * (gt + 1), :]), s["x2i"])
    x2t = Q.x2ring[:, 1024 * slot:1024 * (slot + 1)]
    sc.wait("vector", s["x2i"], c_xi)
    c_st = _emit_ln_stats(sc, T, x2t)
    sc.wait("scalar", s["stat"], c_st)
    c_sd = sc.op("scalar", lambda e: e.activation(
        T.stdv[:], T.mv[:, 1:2], AF.Sqrt, bias=T.eps2[:],
        scale=16129.0), inc=s["std"])
    sc.wait("vector", s["std"], c_sd)
    sc.op("vector", lambda e: e.reciprocal(T.rstd[:], T.stdv[:]))
    sc.drain("vector")
    Q.x3tm_ring.acquire(sc, "vector", gt)
    xb = gt % 2
    c_ap = sc.op("vector", lambda e, x2t=x2t, xb=xb: e.tensor_scalar(
        Q.x3tm[:, 1024 * xb:1024 * (xb + 1)], x2t,
        T.mv[:, 0:1], T.rstd[:], ALU.subtract, ALU.mult), inc=s["app"])
    Q.x2r_ring.release(gt, s["app"], c_ap)
    return (gt, tj, t, c_ap)


def _emit_ffn_ln_tp(sc, T, Q, pend):
    gt, tj, t, c_ap = pend
    s = sc.sems
    xb = gt % 2
    for c in range(8):
        pk = (gt * 8 + c) % 2
        Q.ptT_ring.acquire(sc, "tensor", gt * 8 + c)
        sc.wait("tensor", s["app"], c_ap)
        def tp2(e, xb=xb, c=c, pk=pk):
            return e.transpose(
                Q.ptT[pk][:, 0:128],
                Q.x3tm[:, 1024 * xb + 128 * c:1024 * xb + 128 * (c + 1)],
                T.idb[:])
        c_tp = sc.op("tensor", tp2, inc=s["tp2"])
        sc.wait("scalar", s["tp2"], c_tp)
        if t > 0 and s["umm"].n:
            sc.wait("scalar", s["umm"], HC * t)
        def ev2(e, c=c, tj=tj, pk=pk):
            return e.copy(
                Q.x3fm[:, 512 * c + 128 * tj:512 * c + 128 * (tj + 1)],
                Q.ptT[pk][:, 0:128])
        c_e2 = sc.op("scalar", ev2, inc=s["ev2"])
        Q.ptT_ring.release(gt * 8 + c, s["ev2"], c_e2)
        if c == 7:
            Q.x3tm_ring.release(gt, s["ev2"], c_e2)


def _emit_ffn_A(sc, T, Q, t, hcs, ev2_done):
    s = sc.sems
    ev2_base = ev2_done - 32
    for hc in hcs:
        if t == 0:
            sc.wait("tensor", *T.w_cnt[("u", hc)])
            sc.wait("tensor", *T.w_cnt[("g", hc)])
        Q.pG_ring.acquire(sc, "tensor", 0)
        c_g = None
        for c in range(8):
            sc.wait("tensor", s["ev2"], ev2_base + 24 + c + 1)
            def gmm(e, hc=hc, c=c):
                return e.matmul(
                    Q.pG[:],
                    T.wg_sb[:, 1024 * hc + 128 * c:1024 * hc + 128 * (c + 1)],
                    Q.x3fm[:, 512 * c:512 * (c + 1)],
                    start=(c == 0), stop=(c == 7))
            c_g = sc.op("tensor", gmm, inc=(s["gmm"] if c == 7 else None))
        Q.pU_ring.acquire(sc, "tensor", 0)
        c_u = None
        for c in range(8):
            def umm(e, hc=hc, c=c):
                return e.matmul(
                    Q.pU[:],
                    T.wu_sb[:, 1024 * hc + 128 * c:1024 * hc + 128 * (c + 1)],
                    Q.x3fm[:, 512 * c:512 * (c + 1)],
                    start=(c == 0), stop=(c == 7))
            c_u = sc.op("tensor", umm, inc=(s["umm"] if c == 7 else None))
        sc.wait("scalar", s["gmm"], c_g)
        Q.silu_ring.acquire(sc, "scalar", hc)
        sb = hc % 2
        c_si = sc.op("scalar", lambda e, sb=sb: e.activation(
            Q.silu[:, 512 * sb:512 * (sb + 1)], Q.pG[:], AF.Silu), inc=s["sil"])
        Q.pG_ring.release(0, s["sil"], c_si)
        sc.wait("vector", s["sil"], c_si)
        sc.wait("vector", s["umm"], c_u)
        Q.hT_ring.acquire(sc, "vector", hc)
        def hmul(e, hc=hc, sb=sb):
            return e.tensor_mul(
                Q.hT[:, 512 * hc:512 * (hc + 1)],
                Q.silu[:, 512 * sb:512 * (sb + 1)], Q.pU[:])
        c_h = sc.op("vector", hmul, inc=s["h"])
        Q.pU_ring.release(0, s["h"], c_h)
        Q.silu_ring.release(hc, s["h"], c_h)
        Q.h_cnt[hc] = c_h


def _emit_ffn_B(sc, T, Q, t, half, slots):
    s = sc.sems
    banks = Q.pB if half == 0 else [Q.pG, Q.pU, Q.pB[0], Q.pB[1]]
    rings = Q.pB_ring if half == 0 else [Q.pG_ring, Q.pU_ring,
                                         Q.pB_ring[0], Q.pB_ring[1]]
    h_needed = s["h"].n
    xres_c = {}
    for tjj in range(2):
        tj = 2 * half + tjj
        gt = 4 * t + tj
        Q.xres_ring.acquire(sc, "sync", gt)
        xres_c[tjj] = sc.dma("sync", lambda e, gt=gt, tjj=tjj: e.dma_start(
            out=Q.xres[:, 1024 * (gt % 2):1024 * (gt % 2 + 1)],
            in_=T.x2_d[128 * gt:128 * (gt + 1), :]), s["x2i"])
    for idx, hc in enumerate(range(HC)):
        sc.wait("tensor", s["h"], h_needed)
        if t == 0:
            sc.wait("tensor", *T.w_cnt[("d", hc)])
        for k in range(4):
            tjj, dh = k % 2, k // 2
            tj = 2 * half + tjj
            if idx == 0:
                rings[k].acquire(sc, "tensor", 0)
            last = (idx == HC - 1)
            def bmm(e, hc=hc, k=k, tj=tj, dh=dh, idx=idx, last=last):
                return e.matmul(
                    banks[k][:],
                    Q.hT[:, 512 * hc + 128 * tj:512 * hc + 128 * (tj + 1)],
                    Q.wd_sb[:, 1024 * hc + 512 * dh:1024 * hc + 512 * (dh + 1)],
                    start=(idx == 0), stop=last)
            c_bst = sc.op("tensor", bmm, inc=(s["bst"] if last else None))
    if half == 1:
        for hc in range(HC):
            Q.hT_ring.release(hc, s["bst"], s["bst"].n)
    c_bm = s["bst"].n
    sc.drain("vector")
    for k in range(4):
        tjj, dh = k % 2, k // 2
        tj = 2 * half + tjj
        gt = 4 * t + tj
        x2t = Q.xres[:, 1024 * (gt % 2) + 512 * dh:
                     1024 * (gt % 2) + 512 * (dh + 1)]
        sc.wait("vector", s["x2i"], xres_c[tjj])
        sc.wait("vector", s["bst"], c_bm - 3 + k)
        ob = k % 2
        Q.outr_ring.acquire(sc, "vector", ob)
        def st2(e, k=k, x2t=x2t, ob=ob):
            return e.scalar_tensor_tensor(
                Q.outr[:, 512 * ob:512 * (ob + 1)],
                banks[k][:], 1.0 / 127.0, x2t, ALU.mult, ALU.add)
        c_sv = sc.op("vector", st2, inc=s["stt"])
        sc.wait("sync", s["stt"], c_sv)
        def yst(e, ob=ob, gt=gt, dh=dh):
            return e.dma_start(
                out=T.y_d[128 * gt:128 * (gt + 1), 512 * dh:512 * (dh + 1)],
                in_=Q.outr[:, 512 * ob:512 * (ob + 1)])
        c_y = sc.dma("sync", yst, s["y"])
        Q.outr_ring.release(ob, s["y"], c_y)
        if dh == 1:
            Q.xres_ring.release(gt, s["stt"], c_sv)
        rings[k].release(0, s["stt"], c_sv)


def _emit_phase2(sc, T, Q, stage_limit=4):
    s = sc.sems
    for e in ENGINES:
        sc.wait(e, s["bar"], 4)
        sc.wait(e, s["ot"], s["ot"].n)
    for hc in range(HC):
        sem = s["wA" if hc % 2 else "wB"]
        T.w_cnt[("d", hc)] = (sem, sc.dma(
            "gpsimd", lambda e, hc=hc: e.dma_start(
                out=Q.wd_sb[:, 1024 * hc:1024 * (hc + 1)], in_=T.wd_d[hc]),
            sem))

    slots_by_t = {}
    for t in range(NTC):
        if t == 0:
            slots_by_t[0] = {}
            for tj in range(4):
                pend = _emit_ffn_ln_a(sc, T, Q, 0, tj, slots_by_t[0])
                _emit_ffn_ln_tp(sc, T, Q, pend)
        ev2_done = 32 * (t + 1)
        if stage_limit >= 2:
            _emit_ffn_A(sc, T, Q, t, range(HC), ev2_done)
        pends = []
        if t + 1 < NTC:
            slots_by_t[t + 1] = {}
            for tj in range(2):
                pends.append(
                    _emit_ffn_ln_a(sc, T, Q, t + 1, tj, slots_by_t[t + 1]))
        if stage_limit >= 3:
            _emit_ffn_B(sc, T, Q, t, half=0, slots=slots_by_t[t])
        for pend in pends:
            _emit_ffn_ln_tp(sc, T, Q, pend)
        pends = []
        if t + 1 < NTC:
            for tj in range(2, 4):
                pends.append(
                    _emit_ffn_ln_a(sc, T, Q, t + 1, tj, slots_by_t[t + 1]))
        if stage_limit >= 3:
            _emit_ffn_B(sc, T, Q, t, half=1, slots=slots_by_t[t])
        for pend in pends:
            _emit_ffn_ln_tp(sc, T, Q, pend)
        if stage_limit < 3:
            slots = slots_by_t[t]
            for tj in range(4):
                gt = 4 * t + tj
                slot = slots[tj]
                for dh in range(2):
                    ob = (2 * gt + dh) % 2
                    sc.wait("vector", s["app"], s["app"].n)
                    if stage_limit >= 2:
                        sc.wait("vector", s["h"], s["h"].n)
                    Q.outr_ring.acquire(sc, "vector", ob)
                    c_cp = sc.op("vector", lambda e, slot=slot, ob=ob, dh=dh: e.tensor_copy(
                        Q.outr[:, 512 * ob:512 * (ob + 1)],
                        Q.x2ring[:, 1024 * slot + 512 * dh:1024 * slot + 512 * (dh + 1)]),
                        inc=s["stt"])
                    sc.wait("sync", s["stt"], c_cp)
                    c_y = sc.dma("sync", lambda e, gt=gt, ob=ob, dh=dh: e.dma_start(
                        out=T.y_d[128 * gt:128 * (gt + 1), 512 * dh:512 * (dh + 1)],
                        in_=Q.outr[:, 512 * ob:512 * (ob + 1)]), s["y"])
                    Q.outr_ring.release(ob, s["y"], c_y)
                    if dh == 1:
                        Q.x2r_ring.release(gt, s["stt"], c_cp)

    sc.ops["sync"].append(lambda e: e.wait_ge(s["y"].h, s["y"].n))


def build_program(debug_phase=None):
    nc = bass.Bass("TRN2", target_bir_lowering=False,
                   detect_race_conditions=False)

    T = NS(w_cnt={})
    T.x_d = nc.dram_tensor("x", [TOK, DM], F32, kind="ExternalInput").ap()
    T.wg_d = nc.dram_tensor("wg", [HC, DM, 128], I32, kind="ExternalInput").ap()
    T.wu_d = nc.dram_tensor("wu", [HC, DM, 128], I32, kind="ExternalInput").ap()
    T.wd_d = nc.dram_tensor("wd", [HC, 128, DM], I32, kind="ExternalInput").ap()
    T.idb_d = nc.dram_tensor("idb", [128, 128], BF16, kind="ExternalInput").ap()
    T.tri_d = nc.dram_tensor("tri", [128, 128], BF16, kind="ExternalInput").ap()
    T.y_d = nc.dram_tensor("y", [TOK, DM], F32, kind="ExternalOutput").ap()
    if debug_phase in ("p2", "p2ln", "p2a", "p2b"):
        T.x2_d = nc.dram_tensor("x2in", [TOK, DM], F32, kind="ExternalInput").ap()
    else:
        T.x2_d = nc.dram_tensor("x2s", [TOK, DM], F32, kind="Internal").ap()
    if debug_phase == "p1":
        T.dbg1 = nc.dram_tensor("dbg1", [128, NTB * NH * 65], BF16,
                                kind="ExternalOutput").ap()
        T.dbg2 = nc.dram_tensor("dbg2", [128, 8 * SEQ], BF16,
                                kind="ExternalOutput").ap()
        T.dbg3 = nc.dram_tensor("dbg3", [128, 32], F32,
                                kind="ExternalOutput").ap()
        T.dbg4 = nc.dram_tensor("dbg4", [128, 4096], F32,
                                kind="ExternalOutput").ap()
        T.dbg5 = nc.dram_tensor("dbg5", [128, 2048], BF16,
                                kind="ExternalOutput").ap()
        T.dbg6 = nc.dram_tensor("dbg6", [128, 1024], BF16,
                                kind="ExternalOutput").ap()

    sc = Sched()
    for name in ("x", "wA", "wB", "x2o", "x2i", "y", "stat", "std", "app", "tp1",
                 "lns1", "gp2", "gpm",
                 "ev1", "gv", "gt", "gs", "g", "scr", "exp", "msk", "pvs",
                 "pv", "oev", "ot", "nrm", "x2r", "bar", "b0", "tp2", "ev2",
                 "gmm", "umm", "sil", "h", "bst", "bmm", "stt"):
        sc.sem(name)

    with contextlib.ExitStack() as ctx:
        for nm in ("x", "wA", "wB", "x2o", "x2i", "y"):
            sc.sems[nm].dma = True
        for sm in sc.sems.values():
            sm.h = ctx.enter_context(nc.semaphore(sm.name))

        def sbuf(name, shape, dtype):
            return ctx.enter_context(nc.sbuf_tensor(name, shape, dtype))

        T.wg_sb = sbuf("wg_sb", [128, HC * 1024], BF16)
        T.wu_sb = sbuf("wu_sb", [128, HC * 1024], BF16)
        T.idb = sbuf("idb_sb", [128, 128], BF16)
        T.tri = sbuf("tri_sb", [128, 128], BF16)
        T.ones_row = sbuf("ones_row", [1, 128], BF16)
        T.negG = sbuf("negG", [128, 32], F32)
        T.negGP = sbuf("negGP", [128, 16], F32)
        T.bn6 = sbuf("bn6", [128, 12], F32)
        T.mv = sbuf("mv", [128, 2], F32)
        T.stdv = sbuf("stdv", [128, 1], F32)
        T.rstd = sbuf("rstd", [128, 1], F32)
        T.mv4 = sbuf("mv4", [128, 8], F32)
        T.stdv4 = sbuf("stdv4", [128, 4], F32)
        T.rstd4 = sbuf("rstd4", [128, 4], F32)
        T.m12 = sbuf("m12", [128, 32], F32)
        T.t16 = sbuf("t16", [128, 32], F32)
        T.mbf = sbuf("mbf", [128, 32], BF16)
        T.l1m = sbuf("l1m", [16, 1], F32)
        T.abm = sbuf("abm", [16, 1], F32)
        T.g16 = sbuf("g16", [16, 1], F32)
        T.ng16b = sbuf("ng16b", [16, 1], BF16)
        T.ngrow = sbuf("ngrow", [1, 16], BF16)
        T.rl = sbuf("rl", [128, 8], F32)
        T.lns = sbuf("lns", [128, 8], F32)
        T.eps1 = sbuf("eps1", [128, 1], F32)
        T.eps2 = sbuf("eps2", [128, 1], F32)

        s = sc.sems
        sc.dma("sync", lambda e: e.dma_start(out=T.idb[:], in_=T.idb_d), s["x"])
        c_const = sc.dma("sync", lambda e: e.dma_start(out=T.tri[:], in_=T.tri_d), s["x"])
        sc.wait("vector", s["x"], c_const)
        sc.op("vector", lambda e: e.memset(T.ones_row[:], 1.0))
        sc.op("vector", lambda e: e.memset(T.eps1[:], EPS))
        c_init = sc.op("vector", lambda e: e.memset(T.eps2[:], 16129.0 * EPS), inc=s["gv"])
        sc.wait("scalar", s["gv"], c_init)
        sc.wait("gpsimd", s["gv"], c_init)
        sc.wait("tensor", s["x"], c_const)

        if debug_phase not in ("p1", "p2ln"):
            _queue_weight_loads(sc, T)
            if debug_phase is not None:
                while T.wloads:
                    _emit_next_wload(sc, T)

        with contextlib.ExitStack() as ctx1:
            def sb1(name, shape, dtype):
                return ctx1.enter_context(nc.sbuf_tensor(name, shape, dtype))

            def ps1(name, shape, dtype):
                return ctx1.enter_context(nc.psum_tensor(name, shape, dtype))

            P = NS()
            P.x1tm = sb1("x1tm", [128, NTB * NH * 65 + 64], BF16)
            P.x1fm = sb1("x1fm", [128, 8 * SEQ], BF16)
            P.xring = sb1("xring", [128, 5 * 1024], F32)
            P.stage = sb1("stage", [128, 4096], F32)
            P.pbuf = sb1("pbuf", [128, 2048], BF16)
            otA = sb1("otA", [128, 512], BF16)
            otB = sb1("otB", [128, 512], BF16)
            psP0 = ps1("psP0", [128, 1024], F32)
            psP1 = ps1("psP1", [128, 1024], F32)
            P.poA = ps1("poA", [128, 512], F32)
            poB = ps1("poB", [128, 512], F32)
            P.ptA = ps1("ptA", [128, 512], BF16)
            ptB = ps1("ptB", [128, 512], BF16)

            P.psP = [psP0, psP1]
            P.po = {"A": P.poA, "B": poB}
            P.pt = {"A": P.ptA, "B": ptB}
            P.ot = {"A": otA, "B": otB}
            P.xr = Ring(5)
            P.stg = Ring(4)
            P.psr = [Ring(1), Ring(1)]
            P.pbr = Ring(2)
            P.por = {"A": Ring(1), "B": Ring(1)}
            P.ptr = {"A": Ring(1), "B": Ring(1)}

            if debug_phase not in ("p2", "p2ln", "p2a", "p2b"):
                _emit_phase1(sc, T, P)
                if debug_phase == "p1":
                    sq = sc.sems
                    sc.wait("sync", sq["nrm"], sq["nrm"].n)
                    sc.dma("sync", lambda e: e.dma_start(out=T.dbg1, in_=P.x1tm[:]), sq["y"])
                    sc.dma("sync", lambda e: e.dma_start(out=T.dbg2, in_=P.x1fm[:]), sq["y"])
                    sc.dma("sync", lambda e: e.dma_start(out=T.dbg3, in_=T.negG[:]), sq["y"])
                    sc.dma("sync", lambda e: e.dma_start(out=T.dbg4, in_=P.stage[:]), sq["y"])
                    sc.dma("sync", lambda e: e.dma_start(
                        out=T.dbg5[:, 0:1024], in_=P.pbuf["A"][:]), sq["y"])
                    sc.dma("sync", lambda e: e.dma_start(
                        out=T.dbg5[:, 1024:2048], in_=P.pbuf["B"][:]), sq["y"])
                    sc.dma("sync", lambda e: e.dma_start(
                        out=T.dbg6[:, 0:512], in_=P.ot["A"][:]), sq["y"])
                    sc.dma("sync", lambda e: e.dma_start(
                        out=T.dbg6[:, 512:1024], in_=P.ot["B"][:]), sq["y"])
            else:
                for e in ENGINES:
                    sc.bump(e, sc.sems["bar"])

            with nc.Block() as blk1:
                for ename in ENGINES:
                    getattr(blk1, ename)(
                        lambda eng, fns=list(sc.ops[ename]): [f(eng) for f in fns])
            for ename in ENGINES:
                sc.ops[ename] = []

        with contextlib.ExitStack() as ctx2:
            def sb2(name, shape, dtype):
                return ctx2.enter_context(nc.sbuf_tensor(name, shape, dtype))

            def ps2(name, shape, dtype):
                return ctx2.enter_context(nc.psum_tensor(name, shape, dtype))

            Q = NS(h_cnt={})
            Q.x2ring = sb2("x2ring", [128, 3 * 1024], F32)
            Q.xres = sb2("xres", [128, 2 * 1024], F32)
            Q.hT = sb2("hT", [128, 24 * 512], BF16)
            Q.x3fm = sb2("x3fm", [128, 8 * 512], BF16)
            Q.wd_sb = sb2("wd_sb", [128, HC * 1024], BF16)
            Q.x3tm = sb2("x3tm", [128, 2048], BF16)
            Q.silu = sb2("silu", [128, 2 * 512], BF16)
            Q.outr = sb2("outr", [128, 2 * 512], F32)
            Q.pG = ps2("pG", [128, 512], F32)
            Q.pU = ps2("pU", [128, 512], F32)
            Q.pB = [ps2(f"pB{i}", [128, 512], F32) for i in range(4)]
            Q.ptT = [ps2(f"ptT{i}", [128, 512], BF16) for i in range(2)]

            Q.x2r_ring = Ring(3)
            Q.xres_ring = Ring(2)
            Q.x3tm_ring = Ring(2)
            Q.ptT_ring = Ring(2)
            Q.silu_ring = Ring(2)
            Q.hT_ring = Ring(24)
            Q.outr_ring = Ring(2)
            Q.pG_ring = Ring(1)
            Q.pU_ring = Ring(1)
            Q.pB_ring = [Ring(1) for _ in range(4)]

            if debug_phase in ("p2ln", "p2a", "p2b"):
                _emit_phase2(sc, T, Q, stage_limit={"p2ln": 1, "p2a": 2, "p2b": 3}[debug_phase])
            elif debug_phase == "p1":
                sq = sc.sems
                for e in ENGINES:
                    sc.wait(e, sq["bar"], 5)
                for gt in range(TOK // 128):
                    sc.wait("sync", sq["x2o"], 16 * (gt + 1))
                    slot = gt % 5
                    c_l = sc.dma("sync", lambda e, gt=gt, slot=slot: e.dma_start(
                        out=Q.x2ring[:, 1024 * slot:1024 * (slot + 1)],
                        in_=T.x2_d[128 * gt:128 * (gt + 1), :]), sq["x2i"])
                    sc.wait("sync", sq["x2i"], c_l)
                    sc.dma("sync", lambda e, gt=gt, slot=slot: e.dma_start(
                        out=T.y_d[128 * gt:128 * (gt + 1), :],
                        in_=Q.x2ring[:, 1024 * slot:1024 * (slot + 1)]), sq["y"])
                sc.ops["sync"].append(
                    lambda e: e.wait_ge(sq["y"].h, sq["y"].n))
            else:
                _emit_phase2(sc, T, Q)

            with nc.Block() as blk2:
                for ename in ENGINES:
                    getattr(blk2, ename)(
                        lambda eng, fns=list(sc.ops[ename]): [f(eng) for f in fns])

    return nc


_NC_CACHE = None


def kernel(L0_attn_x, L0_ffn_w_gate, L0_ffn_w_up, L0_ffn_w_down):
    global _NC_CACHE
    x = np.ascontiguousarray(np.asarray(L0_attn_x, dtype=np.float32))
    wg = np.asarray(L0_ffn_w_gate, dtype=np.int32)
    wu = np.asarray(L0_ffn_w_up, dtype=np.int32)
    wd = np.asarray(L0_ffn_w_down, dtype=np.int32)
    B, S, D = x.shape
    assert (B, S, D) == (16, SEQ, DM)

    wg_l = np.ascontiguousarray(
        wg.T.reshape(DM, HC, 128).transpose(1, 0, 2))
    wu_l = np.ascontiguousarray(
        wu.T.reshape(DM, HC, 128).transpose(1, 0, 2))
    wd_l = np.ascontiguousarray(wd.T.reshape(HC, 128, DM))
    idb = np.eye(128, dtype=ml_dtypes.bfloat16)
    tri = np.triu(np.ones((128, 128), dtype=np.float32)).astype(ml_dtypes.bfloat16)

    if _NC_CACHE is None:
        _NC_CACHE = build_program()
    nc = _NC_CACHE

    in_maps = []
    for i in range(NCORES):
        xc = np.ascontiguousarray(x[2 * i:2 * i + 2].reshape(TOK, DM))
        in_maps.append({"x": xc, "wg": wg_l, "wu": wu_l, "wd": wd_l,
                        "idb": idb, "tri": tri})
    res = run_bass_kernel_spmd(nc, in_maps, core_ids=list(range(NCORES)))
    out = np.empty((B, S, D), dtype=np.float32)
    for i in range(NCORES):
        out[2 * i:2 * i + 2] = res.results[i]["y"].reshape(2, SEQ, DM)
    return out


# version 26 (final)
# speedup vs baseline: 1.0363x; 1.0363x over previous
import contextlib
import os
import numpy as np
import ml_dtypes
import concourse.bass as bass
import concourse.mybir as mybir
from concourse.bass_utils import run_bass_kernel_spmd

F32 = mybir.dt.float32
BF16 = mybir.dt.bfloat16
I32 = mybir.dt.int32
AF = mybir.ActivationFunctionType
ALU = mybir.AluOpType
AX = mybir.AxisListType

NCORES = 8
B_CORE = 2
SEQ = 2048
DM = 1024
NH = 16
HD = 64
HID = 3072
NTB = 16
NQC = 4
HC = 24
EPS = 1e-5
TOK = B_CORE * SEQ
NTC = TOK // 512

ENGINES = ("tensor", "vector", "scalar", "sync", "gpsimd")


class Sem:
    def __init__(self, name):
        self.name = name
        self.n = 0
        self.h = None
        self.producer = None
        self.dma = False


class Ring:

    def __init__(self, k):
        self.k = k
        self.guard = [None] * k

    def acquire(self, sched, eng, i):
        g = self.guard[i % self.k]
        if g is not None:
            sched.wait(eng, g[0], g[1])

    def release(self, i, sem, val):
        self.guard[i % self.k] = (sem, val)


class Sched:
    def __init__(self):
        self.ops = {e: [] for e in ENGINES}
        self.sems = {}
        self.last_wait = {e: {} for e in ENGINES}

    def sem(self, name):
        return self.sems.setdefault(name, Sem(name))

    def wait(self, eng, sem, val):
        if val <= 0:
            return
        if sem.producer == eng and not sem.dma:
            return
        if self.last_wait[eng].get(sem.name, -1) >= val:
            return
        self.last_wait[eng][sem.name] = val
        self.ops[eng].append(lambda e, s=sem, v=val: e.wait_ge(s.h, v))

    def op(self, eng, f, inc=None, n=1):
        if inc is not None:
            if inc.producer is None:
                inc.producer = eng
            assert inc.producer == eng, (inc.name, inc.producer, eng)
            inc.n += n
            cnt = inc.n
            self.ops[eng].append(
                lambda e, f=f, s=inc, n=n: f(e).then_inc(s.h, n))
            return cnt
        self.ops[eng].append(f)
        return None

    def dma(self, eng, f, sem, n=16):
        assert sem.dma
        self.wait(eng, sem, sem.n)
        return self.op(eng, f, inc=sem, n=n)

    def drain(self, eng):
        self.ops[eng].append(lambda e: e.drain())

    def bump(self, eng, sem, n=1):
        if sem.producer is None:
            sem.producer = eng
        sem.n += n
        cnt = sem.n
        if eng in ("vector", "scalar"):
            self.ops[eng].append(lambda e, s=sem, n=n: e.drain().then_inc(s.h, n))
        else:
            self.ops[eng].append(lambda e, s=sem, n=n: e.nop().then_inc(s.h, n))
        return cnt


class NS:

    def __init__(self, **kw):
        self.__dict__.update(kw)


def _queue_weight_loads(sc, T):
    T.wloads = []
    for hc in range(HC):
        for nm, dst, srcd in (("g", T.wg_sb, T.wg_d), ("u", T.wu_sb, T.wu_d)):
            def ld(e, dst=dst, srcd=srcd, hc=hc):
                o = dst[:, 1024 * hc:1024 * (hc + 1)].rearrange(
                    "p (c j) -> p c j", j=128)
                i = srcd[hc].rearrange("(c p) j -> p c j", p=128)
                return e.dma_start(out=o, in_=i)
            T.wloads.append((nm, hc, ld))


def _emit_next_wload(sc, T):
    if not T.wloads:
        return
    nm, hc, ld = T.wloads.pop(0)
    sem = sc.sems["wA" if (len(T.wloads) % 2) else "wB"]
    T.w_cnt[(nm, hc)] = (sem, sc.dma("gpsimd", ld, sem))




def _emit_ln_stats(sc, T, xt, mean_ap=None, var_ap=None):
    if mean_ap is None:
        mean_ap = T.mv[:, 0:1]
    if var_ap is None:
        var_ap = T.mv[:, 1:2]
    sc.op("vector", lambda e, xt=xt: e.bn_stats(T.bn6[:, 0:6], xt[:, 0:512]))
    sc.op("vector", lambda e, xt=xt: e.bn_stats(T.bn6[:, 6:12], xt[:, 512:1024]))
    b3 = T.bn6.ap().rearrange("p (c k) -> p c k", k=3)
    means = b3[:, :, 1:2].squeeze(2)
    cvars = b3[:, :, 2:3].squeeze(2)
    L = T.lns
    sc.drain("vector")
    sc.op("vector", lambda e: e.reduce_sum(L[:, 0:1], means, AX.X))
    sc.op("vector", lambda e: e.reduce_sum(L[:, 1:2], cvars, AX.X))
    sc.op("vector", lambda e: e.tensor_mul(L[:, 4:8], means, means))
    sc.drain("vector")
    sc.op("vector", lambda e: e.reduce_sum(L[:, 2:3], L[:, 4:8], AX.X))
    sc.drain("vector")
    sc.op("vector", lambda e: e.tensor_scalar(mean_ap, L[:, 0:1], 0.25,
                                              None, ALU.mult))
    sc.drain("vector")
    sc.op("vector", lambda e: e.tensor_mul(L[:, 3:4], mean_ap, mean_ap))
    sc.drain("vector")
    sc.op("vector", lambda e: e.scalar_tensor_tensor(
        L[:, 3:4], L[:, 2:3], 0.25, L[:, 3:4], ALU.mult, ALU.subtract))
    sc.drain("vector")
    return sc.op("vector", lambda e: e.scalar_tensor_tensor(
        var_ap, L[:, 1:2], 1.0 / 1024.0, L[:, 3:4], ALU.mult, ALU.add),
        inc=sc.sems["stat"])


def _emit_ln_stage1(sc, T, P, b, j, xload):
    s = sc.sems
    gt = b * NTB + j
    t = j % 4
    slot = xload % 5
    P.xr.acquire(sc, "sync", xload)
    c_x = sc.dma("sync", lambda e, gt=gt, slot=slot: e.dma_start(
        out=P.xring[:, 1024 * slot:1024 * (slot + 1)],
        in_=T.x_d[128 * gt:128 * (gt + 1), :]), s["x"])
    xt = P.xring[:, 1024 * slot:1024 * (slot + 1)]
    sc.wait("vector", s["x"], c_x)
    c_st = _emit_ln_stats(sc, T, xt, T.mv4[:, t:t + 1], T.mv4[:, 4 + t:5 + t])
    return (j, slot, xload, c_st), xload + 1


def _emit_ln_stage2(sc, T, P, b, st1):
    s = sc.sems
    j, slot, xl, c_st = st1
    t = j % 4
    xt = P.xring[:, 1024 * slot:1024 * (slot + 1)]
    sc.wait("scalar", s["stat"], c_st)
    c_sd = sc.op("scalar", lambda e, t=t: e.activation(
        T.stdv4[:, t:t + 1], T.mv4[:, 4 + t:5 + t], AF.Sqrt, bias=T.eps1[:],
        scale=1.0), inc=s["std"])
    sc.wait("vector", s["std"], c_sd)
    sc.op("vector", lambda e, t=t: e.reciprocal(
        T.rstd4[:, t:t + 1], T.stdv4[:, t:t + 1]))
    sc.drain("vector")
    x1t3 = P.x1tm[:, 1040 * j:1040 * (j + 1)].rearrange("p (h s) -> p h s", s=65)
    c_ap = sc.op("vector", lambda e, x1t3=x1t3, xt=xt, t=t: e.tensor_scalar(
        x1t3[:, :, 0:64], xt.rearrange("p (h s) -> p h s", s=64),
        T.mv4[:, t:t + 1], T.rstd4[:, t:t + 1], ALU.subtract, ALU.mult),
        inc=s["app"])
    P.xr.release(xl, s["app"], c_ap)
    sc.op("vector", lambda e, x1t3=x1t3: e.memset(x1t3[:, :, 64:65], 1.0))
    sc.drain("vector")
    sc.op("vector", lambda e, x1t3=x1t3: e.reduce_sum(
        T.t16[:, 0:16], x1t3[:, :, 0:64], AX.X, apply_absolute_value=True))
    sc.op("vector", lambda e, x1t3=x1t3: e.reduce_max(
        T.t16[:, 16:32], x1t3[:, :, 0:64], AX.X, apply_absolute_value=True))
    sc.drain("vector")
    if j == 0:
        c_m = sc.op("vector", lambda e: e.tensor_copy(T.m12[:], T.t16[:]),
                    inc=s["gpm"])
    else:
        c_m = sc.op("vector", lambda e: e.tensor_max(T.m12[:], T.m12[:], T.t16[:]),
                    inc=s["gpm"])
    T.m12_cnt = c_m
    return c_ap


def _emit_ln_tp(sc, T, P, b, j, c_ap):
    s = sc.sems
    for c in range(8):
        hk = "A" if (j * 8 + c) % 2 == 0 else "B"
        P.ptr[hk].acquire(sc, "tensor", 0)
        sc.wait("tensor", s["app"], c_ap)
        def tp_e(e, j=j, c=c, hk=hk):
            src = P.x1tm[:, 1040 * j + 65 * (2 * c):1040 * j + 65 * (2 * c) + 64]
            return e.transpose(P.pt[hk][0:64, 0:128], src, T.idb[:])
        sc.op("tensor", tp_e)
        def tp_o(e, j=j, c=c, hk=hk):
            src = P.x1tm[:, 1040 * j + 65 * (2 * c + 1):
                         1040 * j + 65 * (2 * c + 1) + 64]
            return e.transpose(P.pt[hk][64:128, 0:128], src, T.idb[:],
                               tile_position=(0, 64))
        c_tp = sc.op("tensor", tp_o, inc=s["tp1"])
        sc.wait("vector", s["tp1"], c_tp)
        def ev(e, j=j, c=c, hk=hk):
            return e.tensor_copy(
                P.x1fm[:, 2048 * c + 128 * j: 2048 * c + 128 * (j + 1)],
                P.pt[hk][:, 0:128])
        c_ev = sc.op("vector", ev, inc=s["ev1"])
        P.ptr[hk].release(0, s["ev1"], c_ev)


def _emit_g_chain(sc, T, P, b, qc):
    s = sc.sems
    sc.drain("vector")
    c_mb = sc.op("vector", lambda e: e.tensor_copy(T.mbf[:], T.m12[:]), inc=s["gv"])
    P.ptr["A"].acquire(sc, "tensor", 0)
    sc.wait("tensor", s["gv"], c_mb)
    sc.op("tensor", lambda e: e.transpose(
        P.ptA[0:16, 0:128], T.mbf[:, 0:16], T.idb[:]))
    c_g1 = sc.op("tensor", lambda e: e.transpose(
        P.ptA[0:16, 128:256], T.mbf[:, 16:32], T.idb[:]), inc=s["gt"])
    sc.wait("vector", s["gt"], c_g1)
    sc.op("vector", lambda e: e.reduce_max(T.l1m[:], P.ptA[0:16, 0:128], AX.X))
    sc.op("vector", lambda e: e.reduce_max(T.abm[:], P.ptA[0:16, 128:256], AX.X))
    sc.drain("vector")
    sc.op("vector", lambda e: e.tensor_mul(T.g16[:], T.l1m[:], T.abm[:]))
    sc.drain("vector")
    c_g2 = sc.op("vector", lambda e: e.tensor_scalar(
        T.ng16b[:], T.g16[:], -0.125, None, ALU.mult), inc=s["gv"])
    sc.wait("tensor", s["gv"], c_g2)
    c_g3 = sc.op("tensor", lambda e: e.transpose(
        P.ptA[0:1, 256:272], T.ng16b[:], T.idb[0:16, 0:16]), inc=s["gt"])
    sc.wait("scalar", s["gt"], c_g3)
    c_g4 = sc.op("scalar", lambda e: e.copy(
        T.ngrow[:], P.ptA[0:1, 256:272]), inc=s["gs"])
    P.ptr["A"].release(0, s["gs"], c_g4)
    P.por["A"].acquire(sc, "tensor", 0)
    sc.wait("tensor", s["gs"], c_g4)
    c_g5 = sc.op("tensor", lambda e: e.matmul(
        P.poA[0:128, 0:16], T.ones_row[:], T.ngrow[:],
        start=True, stop=True), inc=s["gt"])
    sc.wait("vector", s["gt"], c_g5)
    col = 16 * ((b * NQC + qc) % 2)
    sc.wait("vector", s["exp"], s["exp"].n)
    sc.op("vector", lambda e, col=col: e.tensor_copy(
        T.negG[:, col:col + 16], P.poA[:, 0:16]))
    sc.drain("vector")
    g3 = T.negG[:, col:col + 16].rearrange("p (h two) -> p h two", two=2)
    pc = 8 * ((b * NQC + qc) % 2)
    c_g6 = sc.op("vector", lambda e, g3=g3, pc=pc: e.tensor_tensor(
        T.negGP[:, pc:pc + 8], g3[:, :, 0:1].squeeze(2),
        g3[:, :, 1:2].squeeze(2), ALU.min), inc=s["g"])
    P.por["A"].release(0, s["g"], c_g6)


def _emit_attn_unit(sc, T, P, b, qc, pair, ev1_gate):
    s = sc.sems
    pcol = 8 * ((b * NQC + qc) % 2) + pair
    c_gq = s["g"].n
    kjs = list(range(4 * qc + 4))
    K = len(kjs)
    exp_at, msk_at, rel_of = {}, {}, {}
    pv_stop = {}

    def emit_pv(i, kj):
        rel = rel_of[i]
        ql = 512 - rel
        for hk, hid, hoff in (("A", 2 * pair, 0), ("B", 2 * pair + 1, 512)):
            if kj >= 4 * qc:
                sc.wait("tensor", s["msk"], msk_at[(i, hk)])
            else:
                sc.wait("tensor", s["exp"], exp_at[i])
            if i == 0:
                P.por[hk].acquire(sc, "tensor", 0)
            def pv(e, hk=hk, hid=hid, kj=kj, i=i, rel=rel, ql=ql, hoff=hoff):
                va = P.x1tm[:, 1040 * kj + 65 * hid:1040 * kj + 65 * hid + 128]
                return e.matmul(
                    P.po[hk][0:128, rel:rel + ql], va,
                    P.pbuf[:, 1024 * (i % 2) + hoff + rel:
                           1024 * (i % 2) + hoff + 512],
                    start=(i == 0), stop=(i == K - 1))
            cnt = sc.op("tensor", pv, inc=s["pvs"])
            if hk == "B":
                P.pbr.release(i, s["pvs"], cnt)
            if i == K - 1:
                pv_stop[hk] = cnt

    for i, kj in enumerate(kjs):
        sb = i % 2
        rel_of[i] = max(0, 128 * kj - 512 * qc)
        sc.wait("tensor", s["ev1"], ev1_gate)
        for hk, r0, tpos, hoff in (("A", 0, (0, 0), 0),
                                   ("B", 64, (64, 0), 512)):
            P.psr[sb].acquire(sc, "tensor", 0)
            def smm(e, r0=r0, tpos=tpos, pair=pair, kj=kj, qc=qc, sb=sb,
                    hoff=hoff):
                lhs = P.x1fm[r0:r0 + 64,
                             2048 * pair + 128 * kj:2048 * pair + 128 * (kj + 1)]
                rhs = P.x1fm[r0:r0 + 64,
                             2048 * pair + 512 * qc:2048 * pair + 512 * (qc + 1)]
                return e.matmul(P.psP[sb][:, hoff:hoff + 512], lhs, rhs,
                                start=True, stop=True, tile_position=tpos)
            c_s = sc.op("tensor", smm, inc=(s["scr"] if hk == "B" else None))
        sc.wait("scalar", s["scr"], c_s)
        sc.wait("scalar", s["g"], c_gq)
        P.pbr.acquire(sc, "scalar", i)
        rel = rel_of[i]
        if rel == 0:
            def ex(e, sb=sb, pcol=pcol):
                return e.activation(
                    P.pbuf[:, 1024 * sb:1024 * sb + 1024],
                    P.psP[sb][:, 0:1024], AF.Exp,
                    bias=T.negGP[:, pcol:pcol + 1], scale=0.125)
            c_e = sc.op("scalar", ex, inc=s["exp"])
        else:
            for hoff in (0, 512):
                def ex(e, sb=sb, pcol=pcol, hoff=hoff, rel=rel):
                    return e.activation(
                        P.pbuf[:, 1024 * sb + hoff + rel:1024 * sb + hoff + 512],
                        P.psP[sb][:, hoff + rel:hoff + 512], AF.Exp,
                        bias=T.negGP[:, pcol:pcol + 1], scale=0.125)
                c_e = sc.op("scalar", ex, inc=s["exp"])
        exp_at[i] = c_e
        P.psr[sb].release(0, s["exp"], c_e)
        if kj >= 4 * qc:
            sc.wait("gpsimd", s["exp"], c_e)
            for hk, hoff in (("A", 0), ("B", 512)):
                def mk(e, sb=sb, hoff=hoff, rel=rel):
                    o = 1024 * sb + hoff + rel
                    return e.tensor_mul(P.pbuf[:, o:o + 128],
                                        P.pbuf[:, o:o + 128], T.tri[:])
                msk_at[(i, hk)] = sc.op("gpsimd", mk, inc=s["msk"])
        if i >= 1:
            emit_pv(i - 1, kjs[i - 1])
    emit_pv(K - 1, kjs[K - 1])

    c_oe = {}
    for hk in ("A", "B"):
        sc.wait("vector", s["pvs"], pv_stop[hk])
        c_oe[hk] = sc.op("vector", lambda e, hk=hk: e.tensor_copy(
            P.ot[hk][0:65, :], P.po[hk][0:65, :]), inc=s["oev"])
        P.por[hk].release(0, s["oev"], c_oe[hk])
    for hk, hid in (("A", 2 * pair), ("B", 2 * pair + 1)):
        sc.wait("tensor", s["oev"], c_oe[hk])
        P.ptr[hk].acquire(sc, "tensor", 0)
        c_ot = None
        for c in range(4):
            def otp(e, hk=hk, c=c):
                return e.transpose(
                    P.pt[hk][:, 66 * c:66 * c + 65],
                    P.ot[hk][0:65, 128 * c:128 * (c + 1)], T.idb[0:65, 0:65])
            c_ot = sc.op("tensor", otp, inc=s["ot"])
        sc.wait("vector", s["ot"], c_ot)
        pt3 = P.pt[hk][:, 0:264].rearrange("p (c s) -> p c s", s=66)
        roff = 0 if hk == "A" else 4
        sc.op("vector", lambda e, pt3=pt3, roff=roff: e.reciprocal(
            T.rl[:, roff:roff + 4], pt3[:, 0:4, 64:65].squeeze(2)))
        sc.drain("vector")
        c_nm = None
        for c in range(4):
            P.stg.acquire(sc, "vector", c)
            def nrm(e, pt3=pt3, c=c, hid=hid, roff=roff):
                return e.tensor_scalar(
                    P.stage[:, 1024 * c + 64 * hid:1024 * c + 64 * (hid + 1)],
                    pt3[:, c:c + 1, 0:64].squeeze(1),
                    T.rl[:, roff + c:roff + c + 1], None, ALU.mult)
            c_nm = sc.op("vector", nrm, inc=s["nrm"])
        P.ptr[hk].release(0, s["nrm"], c_nm)


def _emit_res_loads(sc, T, P, b, qc, xload):
    s = sc.sems
    info = []
    for cc in range(4):
        gt = b * NTB + qc * 4 + cc
        slot = xload % 5
        P.xr.acquire(sc, "sync", xload)
        c_x = sc.dma("sync", lambda e, gt=gt, slot=slot: e.dma_start(
            out=P.xring[:, 1024 * slot:1024 * (slot + 1)],
            in_=T.x_d[128 * gt:128 * (gt + 1), :]), s["x"])
        info.append((slot, c_x, xload))
        xload += 1
    return info, xload


def _emit_residual_qc(sc, T, P, b, qc, res_info):
    s = sc.sems
    sc.wait("gpsimd", s["nrm"], s["nrm"].n)
    for cc in range(4):
        gt = b * NTB + qc * 4 + cc
        slot, c_x, xl = res_info[cc]
        sc.wait("gpsimd", s["x"], c_x)
        def radd(e, slot=slot, cc=cc):
            st = P.stage[:, 1024 * cc:1024 * (cc + 1)]
            return e.tensor_add(st, st,
                                P.xring[:, 1024 * slot:1024 * (slot + 1)])
        c_r = sc.op("gpsimd", radd, inc=s["x2r"])
        P.xr.release(xl, s["x2r"], c_r)
        sc.wait("sync", s["x2r"], c_r)
        c_so = sc.dma("sync", lambda e, gt=gt, cc=cc: e.dma_start(
            out=T.x2_d[128 * gt:128 * (gt + 1), :],
            in_=P.stage[:, 1024 * cc:1024 * (cc + 1)]), s["x2o"])
        P.stg.release(cc, s["x2o"], c_so)


def _emit_phase1(sc, T, P):
    s = sc.sems
    xload = 0
    ot_total = 0
    sc.op("vector", lambda e: e.memset(P.x1tm[:, NTB * NH * 65:], 0.0))
    for b in range(B_CORE):
        ev1_at_tile = {}
        st1s = []
        for j in range(4):
            st1, xload = _emit_ln_stage1(sc, T, P, b, j, xload)
            st1s.append(st1)
        if b > 0:
            sc.wait("vector", s["ot"], ot_total)
        for st1 in st1s:
            c_ap = _emit_ln_stage2(sc, T, P, b, st1)
            _emit_ln_tp(sc, T, P, b, st1[0], c_ap)
            ev1_at_tile[st1[0]] = s["ev1"].n
        _emit_g_chain(sc, T, P, b, 0)
        for qc in range(NQC):
            pending_s2 = None
            pending_tp = None
            for pair in range(8):
                if pair == 6:
                    res_info, xload = _emit_res_loads(sc, T, P, b, qc, xload)
                _emit_attn_unit(sc, T, P, b, qc, pair, ev1_at_tile[4 * qc + 3])
                if pending_tp is not None:
                    j, c_ap = pending_tp
                    _emit_ln_tp(sc, T, P, b, j, c_ap)
                    ev1_at_tile[j] = s["ev1"].n
                    pending_tp = None
                if pending_s2 is not None:
                    c_ap = _emit_ln_stage2(sc, T, P, b, pending_s2)
                    pending_tp = (pending_s2[0], c_ap)
                    if pending_s2[0] % 4 == 3:
                        _emit_g_chain(sc, T, P, b, qc + 1)
                    pending_s2 = None
                _emit_next_wload(sc, T)
                if pair < 4 and qc < NQC - 1:
                    j = 4 * (qc + 1) + pair
                    pending_s2, xload = _emit_ln_stage1(sc, T, P, b, j, xload)
            if pending_tp is not None:
                j, c_ap = pending_tp
                _emit_ln_tp(sc, T, P, b, j, c_ap)
                ev1_at_tile[j] = s["ev1"].n
            _emit_residual_qc(sc, T, P, b, qc, res_info)
        if b == 0:
            ot_total = s["ot"].n
    while getattr(T, "wloads", None):
        _emit_next_wload(sc, T)
    for e in ENGINES:
        if e == "tensor":
            continue
        sc.bump(e, s["bar"])


def _emit_ffn_ln_a(sc, T, Q, t, tj, slots):
    s = sc.sems
    gt = 4 * t + tj
    slot = gt % 3
    slots[tj] = slot
    Q.x2r_ring.acquire(sc, "sync", gt)
    if s["x2o"].n:
        sc.wait("sync", s["x2o"], 16 * (gt + 1))
    c_xi = sc.dma("sync", lambda e, gt=gt, slot=slot: e.dma_start(
        out=Q.x2ring[:, 1024 * slot:1024 * (slot + 1)],
        in_=T.x2_d[128 * gt:128 * (gt + 1), :]), s["x2i"])
    x2t = Q.x2ring[:, 1024 * slot:1024 * (slot + 1)]
    sc.wait("vector", s["x2i"], c_xi)
    c_st = _emit_ln_stats(sc, T, x2t)
    sc.wait("scalar", s["stat"], c_st)
    c_sd = sc.op("scalar", lambda e: e.activation(
        T.stdv[:], T.mv[:, 1:2], AF.Sqrt, bias=T.eps2[:],
        scale=16129.0), inc=s["std"])
    sc.wait("vector", s["std"], c_sd)
    sc.op("vector", lambda e: e.reciprocal(T.rstd[:], T.stdv[:]))
    sc.drain("vector")
    Q.x3tm_ring.acquire(sc, "vector", gt)
    xb = gt % 2
    c_ap = sc.op("vector", lambda e, x2t=x2t, xb=xb: e.tensor_scalar(
        Q.x3tm[:, 1024 * xb:1024 * (xb + 1)], x2t,
        T.mv[:, 0:1], T.rstd[:], ALU.subtract, ALU.mult), inc=s["app"])
    Q.x2r_ring.release(gt, s["app"], c_ap)
    return (gt, tj, t, c_ap)


def _emit_ffn_ln_tp(sc, T, Q, pend):
    gt, tj, t, c_ap = pend
    s = sc.sems
    xb = gt % 2
    for c in range(8):
        pk = (gt * 8 + c) % 2
        Q.ptT_ring.acquire(sc, "tensor", gt * 8 + c)
        sc.wait("tensor", s["app"], c_ap)
        def tp2(e, xb=xb, c=c, pk=pk):
            return e.transpose(
                Q.ptT[pk][:, 0:128],
                Q.x3tm[:, 1024 * xb + 128 * c:1024 * xb + 128 * (c + 1)],
                T.idb[:])
        c_tp = sc.op("tensor", tp2, inc=s["tp2"])
        sc.wait("scalar", s["tp2"], c_tp)
        if t > 0 and s["umm"].n:
            sc.wait("scalar", s["umm"], HC * t)
        def ev2(e, c=c, tj=tj, pk=pk):
            return e.copy(
                Q.x3fm[:, 512 * c + 128 * tj:512 * c + 128 * (tj + 1)],
                Q.ptT[pk][:, 0:128])
        c_e2 = sc.op("scalar", ev2, inc=s["ev2"])
        Q.ptT_ring.release(gt * 8 + c, s["ev2"], c_e2)
        if c == 7:
            Q.x3tm_ring.release(gt, s["ev2"], c_e2)


def _emit_ffn_A(sc, T, Q, t, hcs, ev2_done):
    s = sc.sems
    ev2_base = ev2_done - 32
    for hc in hcs:
        if t == 0:
            sc.wait("tensor", *T.w_cnt[("u", hc)])
            sc.wait("tensor", *T.w_cnt[("g", hc)])
        Q.pG_ring.acquire(sc, "tensor", 0)
        c_g = None
        for c in range(8):
            sc.wait("tensor", s["ev2"], ev2_base + 24 + c + 1)
            def gmm(e, hc=hc, c=c):
                return e.matmul(
                    Q.pG[:],
                    T.wg_sb[:, 1024 * hc + 128 * c:1024 * hc + 128 * (c + 1)],
                    Q.x3fm[:, 512 * c:512 * (c + 1)],
                    start=(c == 0), stop=(c == 7))
            c_g = sc.op("tensor", gmm, inc=(s["gmm"] if c == 7 else None))
        Q.pU_ring.acquire(sc, "tensor", 0)
        c_u = None
        for c in range(8):
            def umm(e, hc=hc, c=c):
                return e.matmul(
                    Q.pU[:],
                    T.wu_sb[:, 1024 * hc + 128 * c:1024 * hc + 128 * (c + 1)],
                    Q.x3fm[:, 512 * c:512 * (c + 1)],
                    start=(c == 0), stop=(c == 7))
            c_u = sc.op("tensor", umm, inc=(s["umm"] if c == 7 else None))
        sc.wait("scalar", s["gmm"], c_g)
        Q.silu_ring.acquire(sc, "scalar", hc)
        sb = hc % 2
        c_si = sc.op("scalar", lambda e, sb=sb: e.activation(
            Q.silu[:, 512 * sb:512 * (sb + 1)], Q.pG[:], AF.Silu), inc=s["sil"])
        Q.pG_ring.release(0, s["sil"], c_si)
        sc.wait("vector", s["sil"], c_si)
        sc.wait("vector", s["umm"], c_u)
        Q.hT_ring.acquire(sc, "vector", hc)
        def hmul(e, hc=hc, sb=sb):
            return e.tensor_mul(
                Q.hT[:, 512 * hc:512 * (hc + 1)],
                Q.silu[:, 512 * sb:512 * (sb + 1)], Q.pU[:])
        c_h = sc.op("vector", hmul, inc=s["h"])
        Q.pU_ring.release(0, s["h"], c_h)
        Q.silu_ring.release(hc, s["h"], c_h)
        Q.h_cnt[hc] = c_h


def _emit_ffn_B(sc, T, Q, t, half, slots):
    s = sc.sems
    banks = Q.pB if half == 0 else [Q.pG, Q.pU, Q.pB[0], Q.pB[1]]
    rings = Q.pB_ring if half == 0 else [Q.pG_ring, Q.pU_ring,
                                         Q.pB_ring[0], Q.pB_ring[1]]
    h_needed = s["h"].n
    xres_c = {}
    for tjj in range(2):
        tj = 2 * half + tjj
        gt = 4 * t + tj
        Q.xres_ring.acquire(sc, "sync", gt)
        xres_c[tjj] = sc.dma("sync", lambda e, gt=gt, tjj=tjj: e.dma_start(
            out=Q.xres[:, 1024 * (gt % 2):1024 * (gt % 2 + 1)],
            in_=T.x2_d[128 * gt:128 * (gt + 1), :]), s["x2i"])
    for idx, hc in enumerate(range(HC)):
        sc.wait("tensor", s["h"], h_needed)
        if t == 0:
            sc.wait("tensor", *T.w_cnt[("d", hc)])
        for k in range(4):
            tjj, dh = k % 2, k // 2
            tj = 2 * half + tjj
            if idx == 0:
                rings[k].acquire(sc, "tensor", 0)
            last = (idx == HC - 1)
            def bmm(e, hc=hc, k=k, tj=tj, dh=dh, idx=idx, last=last):
                return e.matmul(
                    banks[k][:],
                    Q.hT[:, 512 * hc + 128 * tj:512 * hc + 128 * (tj + 1)],
                    Q.wd_sb[:, 1024 * hc + 512 * dh:1024 * hc + 512 * (dh + 1)],
                    start=(idx == 0), stop=last)
            c_bst = sc.op("tensor", bmm, inc=(s["bst"] if last else None))
    if half == 1:
        for hc in range(HC):
            Q.hT_ring.release(hc, s["bst"], s["bst"].n)
    c_bm = s["bst"].n
    sc.drain("vector")
    for k in range(4):
        tjj, dh = k % 2, k // 2
        tj = 2 * half + tjj
        gt = 4 * t + tj
        x2t = Q.xres[:, 1024 * (gt % 2) + 512 * dh:
                     1024 * (gt % 2) + 512 * (dh + 1)]
        sc.wait("vector", s["x2i"], xres_c[tjj])
        sc.wait("vector", s["bst"], c_bm - 3 + k)
        ob = k % 2
        Q.outr_ring.acquire(sc, "vector", ob)
        def st2(e, k=k, x2t=x2t, ob=ob):
            return e.scalar_tensor_tensor(
                Q.outr[:, 512 * ob:512 * (ob + 1)],
                banks[k][:], 1.0 / 127.0, x2t, ALU.mult, ALU.add)
        c_sv = sc.op("vector", st2, inc=s["stt"])
        sc.wait("sync", s["stt"], c_sv)
        def yst(e, ob=ob, gt=gt, dh=dh):
            return e.dma_start(
                out=T.y_d[128 * gt:128 * (gt + 1), 512 * dh:512 * (dh + 1)],
                in_=Q.outr[:, 512 * ob:512 * (ob + 1)])
        c_y = sc.dma("sync", yst, s["y"])
        Q.outr_ring.release(ob, s["y"], c_y)
        if dh == 1:
            Q.xres_ring.release(gt, s["stt"], c_sv)
        rings[k].release(0, s["stt"], c_sv)


def _emit_phase2(sc, T, Q, stage_limit=4):
    s = sc.sems
    for e in ENGINES:
        sc.wait(e, s["bar"], 4)
        sc.wait(e, s["ot"], s["ot"].n)
    for hc in range(HC):
        sem = s["wA" if hc % 2 else "wB"]
        T.w_cnt[("d", hc)] = (sem, sc.dma(
            "gpsimd", lambda e, hc=hc: e.dma_start(
                out=Q.wd_sb[:, 1024 * hc:1024 * (hc + 1)], in_=T.wd_d[hc]),
            sem))

    slots_by_t = {}
    for t in range(NTC):
        if t == 0:
            slots_by_t[0] = {}
            for tj in range(4):
                pend = _emit_ffn_ln_a(sc, T, Q, 0, tj, slots_by_t[0])
                _emit_ffn_ln_tp(sc, T, Q, pend)
        ev2_done = 32 * (t + 1)
        if stage_limit >= 2:
            _emit_ffn_A(sc, T, Q, t, range(HC), ev2_done)
        pends = []
        if t + 1 < NTC:
            slots_by_t[t + 1] = {}
            for tj in range(2):
                pends.append(
                    _emit_ffn_ln_a(sc, T, Q, t + 1, tj, slots_by_t[t + 1]))
        if stage_limit >= 3:
            _emit_ffn_B(sc, T, Q, t, half=0, slots=slots_by_t[t])
        for pend in pends:
            _emit_ffn_ln_tp(sc, T, Q, pend)
        pends = []
        if t + 1 < NTC:
            for tj in range(2, 4):
                pends.append(
                    _emit_ffn_ln_a(sc, T, Q, t + 1, tj, slots_by_t[t + 1]))
        if stage_limit >= 3:
            _emit_ffn_B(sc, T, Q, t, half=1, slots=slots_by_t[t])
        for pend in pends:
            _emit_ffn_ln_tp(sc, T, Q, pend)
        if stage_limit < 3:
            slots = slots_by_t[t]
            for tj in range(4):
                gt = 4 * t + tj
                slot = slots[tj]
                for dh in range(2):
                    ob = (2 * gt + dh) % 2
                    sc.wait("vector", s["app"], s["app"].n)
                    if stage_limit >= 2:
                        sc.wait("vector", s["h"], s["h"].n)
                    Q.outr_ring.acquire(sc, "vector", ob)
                    c_cp = sc.op("vector", lambda e, slot=slot, ob=ob, dh=dh: e.tensor_copy(
                        Q.outr[:, 512 * ob:512 * (ob + 1)],
                        Q.x2ring[:, 1024 * slot + 512 * dh:1024 * slot + 512 * (dh + 1)]),
                        inc=s["stt"])
                    sc.wait("sync", s["stt"], c_cp)
                    c_y = sc.dma("sync", lambda e, gt=gt, ob=ob, dh=dh: e.dma_start(
                        out=T.y_d[128 * gt:128 * (gt + 1), 512 * dh:512 * (dh + 1)],
                        in_=Q.outr[:, 512 * ob:512 * (ob + 1)]), s["y"])
                    Q.outr_ring.release(ob, s["y"], c_y)
                    if dh == 1:
                        Q.x2r_ring.release(gt, s["stt"], c_cp)

    sc.ops["sync"].append(lambda e: e.wait_ge(s["y"].h, s["y"].n))


def build_program(debug_phase=None):
    nc = bass.Bass("TRN2", target_bir_lowering=False,
                   detect_race_conditions=False)

    T = NS(w_cnt={})
    T.x_d = nc.dram_tensor("x", [TOK, DM], F32, kind="ExternalInput").ap()
    T.wg_d = nc.dram_tensor("wg", [HC, DM, 128], I32, kind="ExternalInput").ap()
    T.wu_d = nc.dram_tensor("wu", [HC, DM, 128], I32, kind="ExternalInput").ap()
    T.wd_d = nc.dram_tensor("wd", [HC, 128, DM], I32, kind="ExternalInput").ap()
    T.idb_d = nc.dram_tensor("idb", [128, 128], BF16, kind="ExternalInput").ap()
    T.tri_d = nc.dram_tensor("tri", [128, 128], BF16, kind="ExternalInput").ap()
    T.y_d = nc.dram_tensor("y", [TOK, DM], F32, kind="ExternalOutput").ap()
    if debug_phase in ("p2", "p2ln", "p2a", "p2b"):
        T.x2_d = nc.dram_tensor("x2in", [TOK, DM], F32, kind="ExternalInput").ap()
    else:
        T.x2_d = nc.dram_tensor("x2s", [TOK, DM], F32, kind="Internal").ap()
    if debug_phase == "p1":
        T.dbg1 = nc.dram_tensor("dbg1", [128, NTB * NH * 65], BF16,
                                kind="ExternalOutput").ap()
        T.dbg2 = nc.dram_tensor("dbg2", [128, 8 * SEQ], BF16,
                                kind="ExternalOutput").ap()
        T.dbg3 = nc.dram_tensor("dbg3", [128, 32], F32,
                                kind="ExternalOutput").ap()
        T.dbg4 = nc.dram_tensor("dbg4", [128, 4096], F32,
                                kind="ExternalOutput").ap()
        T.dbg5 = nc.dram_tensor("dbg5", [128, 2048], BF16,
                                kind="ExternalOutput").ap()
        T.dbg6 = nc.dram_tensor("dbg6", [128, 1024], BF16,
                                kind="ExternalOutput").ap()

    sc = Sched()
    for name in ("x", "wA", "wB", "x2o", "x2i", "y", "stat", "std", "app", "tp1",
                 "lns1", "gp2", "gpm",
                 "ev1", "gv", "gt", "gs", "g", "scr", "exp", "msk", "pvs",
                 "pv", "oev", "ot", "nrm", "x2r", "bar", "b0", "tp2", "ev2",
                 "gmm", "umm", "sil", "h", "bst", "bmm", "stt"):
        sc.sem(name)

    with contextlib.ExitStack() as ctx:
        for nm in ("x", "wA", "wB", "x2o", "x2i", "y"):
            sc.sems[nm].dma = True
        for sm in sc.sems.values():
            sm.h = ctx.enter_context(nc.semaphore(sm.name))

        def sbuf(name, shape, dtype):
            return ctx.enter_context(nc.sbuf_tensor(name, shape, dtype))

        T.wg_sb = sbuf("wg_sb", [128, HC * 1024], BF16)
        T.wu_sb = sbuf("wu_sb", [128, HC * 1024], BF16)
        T.idb = sbuf("idb_sb", [128, 128], BF16)
        T.tri = sbuf("tri_sb", [128, 128], BF16)
        T.ones_row = sbuf("ones_row", [1, 128], BF16)
        T.negG = sbuf("negG", [128, 32], F32)
        T.negGP = sbuf("negGP", [128, 16], F32)
        T.bn6 = sbuf("bn6", [128, 12], F32)
        T.mv = sbuf("mv", [128, 2], F32)
        T.stdv = sbuf("stdv", [128, 1], F32)
        T.rstd = sbuf("rstd", [128, 1], F32)
        T.mv4 = sbuf("mv4", [128, 8], F32)
        T.stdv4 = sbuf("stdv4", [128, 4], F32)
        T.rstd4 = sbuf("rstd4", [128, 4], F32)
        T.m12 = sbuf("m12", [128, 32], F32)
        T.t16 = sbuf("t16", [128, 32], F32)
        T.mbf = sbuf("mbf", [128, 32], BF16)
        T.l1m = sbuf("l1m", [16, 1], F32)
        T.abm = sbuf("abm", [16, 1], F32)
        T.g16 = sbuf("g16", [16, 1], F32)
        T.ng16b = sbuf("ng16b", [16, 1], BF16)
        T.ngrow = sbuf("ngrow", [1, 16], BF16)
        T.rl = sbuf("rl", [128, 8], F32)
        T.lns = sbuf("lns", [128, 8], F32)
        T.eps1 = sbuf("eps1", [128, 1], F32)
        T.eps2 = sbuf("eps2", [128, 1], F32)

        s = sc.sems
        sc.dma("sync", lambda e: e.dma_start(out=T.idb[:], in_=T.idb_d), s["x"])
        c_const = sc.dma("sync", lambda e: e.dma_start(out=T.tri[:], in_=T.tri_d), s["x"])
        sc.wait("vector", s["x"], c_const)
        sc.op("vector", lambda e: e.memset(T.ones_row[:], 1.0))
        sc.op("vector", lambda e: e.memset(T.eps1[:], EPS))
        c_init = sc.op("vector", lambda e: e.memset(T.eps2[:], 16129.0 * EPS), inc=s["gv"])
        sc.wait("scalar", s["gv"], c_init)
        sc.wait("gpsimd", s["gv"], c_init)
        sc.wait("tensor", s["x"], c_const)

        if debug_phase not in ("p1", "p2ln"):
            _queue_weight_loads(sc, T)
            if debug_phase is not None:
                while T.wloads:
                    _emit_next_wload(sc, T)

        with contextlib.ExitStack() as ctx1:
            def sb1(name, shape, dtype):
                return ctx1.enter_context(nc.sbuf_tensor(name, shape, dtype))

            def ps1(name, shape, dtype):
                return ctx1.enter_context(nc.psum_tensor(name, shape, dtype))

            P = NS()
            P.x1tm = sb1("x1tm", [128, NTB * NH * 65 + 64], BF16)
            P.x1fm = sb1("x1fm", [128, 8 * SEQ], BF16)
            P.xring = sb1("xring", [128, 5 * 1024], F32)
            P.stage = sb1("stage", [128, 4096], F32)
            P.pbuf = sb1("pbuf", [128, 2048], BF16)
            otA = sb1("otA", [128, 512], BF16)
            otB = sb1("otB", [128, 512], BF16)
            psP0 = ps1("psP0", [128, 1024], F32)
            psP1 = ps1("psP1", [128, 1024], F32)
            P.poA = ps1("poA", [128, 512], F32)
            poB = ps1("poB", [128, 512], F32)
            P.ptA = ps1("ptA", [128, 512], BF16)
            ptB = ps1("ptB", [128, 512], BF16)

            P.psP = [psP0, psP1]
            P.po = {"A": P.poA, "B": poB}
            P.pt = {"A": P.ptA, "B": ptB}
            P.ot = {"A": otA, "B": otB}
            P.xr = Ring(5)
            P.stg = Ring(4)
            P.psr = [Ring(1), Ring(1)]
            P.pbr = Ring(2)
            P.por = {"A": Ring(1), "B": Ring(1)}
            P.ptr = {"A": Ring(1), "B": Ring(1)}

            if debug_phase not in ("p2", "p2ln", "p2a", "p2b"):
                _emit_phase1(sc, T, P)
                if debug_phase == "p1":
                    sq = sc.sems
                    sc.wait("sync", sq["nrm"], sq["nrm"].n)
                    sc.dma("sync", lambda e: e.dma_start(out=T.dbg1, in_=P.x1tm[:]), sq["y"])
                    sc.dma("sync", lambda e: e.dma_start(out=T.dbg2, in_=P.x1fm[:]), sq["y"])
                    sc.dma("sync", lambda e: e.dma_start(out=T.dbg3, in_=T.negG[:]), sq["y"])
                    sc.dma("sync", lambda e: e.dma_start(out=T.dbg4, in_=P.stage[:]), sq["y"])
                    sc.dma("sync", lambda e: e.dma_start(
                        out=T.dbg5[:, 0:1024], in_=P.pbuf["A"][:]), sq["y"])
                    sc.dma("sync", lambda e: e.dma_start(
                        out=T.dbg5[:, 1024:2048], in_=P.pbuf["B"][:]), sq["y"])
                    sc.dma("sync", lambda e: e.dma_start(
                        out=T.dbg6[:, 0:512], in_=P.ot["A"][:]), sq["y"])
                    sc.dma("sync", lambda e: e.dma_start(
                        out=T.dbg6[:, 512:1024], in_=P.ot["B"][:]), sq["y"])
            else:
                for e in ENGINES:
                    sc.bump(e, sc.sems["bar"])

            with nc.Block() as blk1:
                for ename in ENGINES:
                    getattr(blk1, ename)(
                        lambda eng, fns=list(sc.ops[ename]): [f(eng) for f in fns])
            for ename in ENGINES:
                sc.ops[ename] = []

        with contextlib.ExitStack() as ctx2:
            def sb2(name, shape, dtype):
                return ctx2.enter_context(nc.sbuf_tensor(name, shape, dtype))

            def ps2(name, shape, dtype):
                return ctx2.enter_context(nc.psum_tensor(name, shape, dtype))

            Q = NS(h_cnt={})
            Q.x2ring = sb2("x2ring", [128, 3 * 1024], F32)
            Q.xres = sb2("xres", [128, 2 * 1024], F32)
            Q.hT = sb2("hT", [128, 24 * 512], BF16)
            Q.x3fm = sb2("x3fm", [128, 8 * 512], BF16)
            Q.wd_sb = sb2("wd_sb", [128, HC * 1024], BF16)
            Q.x3tm = sb2("x3tm", [128, 2048], BF16)
            Q.silu = sb2("silu", [128, 2 * 512], BF16)
            Q.outr = sb2("outr", [128, 2 * 512], F32)
            Q.pG = ps2("pG", [128, 512], F32)
            Q.pU = ps2("pU", [128, 512], F32)
            Q.pB = [ps2(f"pB{i}", [128, 512], F32) for i in range(4)]
            Q.ptT = [ps2(f"ptT{i}", [128, 512], BF16) for i in range(2)]

            Q.x2r_ring = Ring(3)
            Q.xres_ring = Ring(2)
            Q.x3tm_ring = Ring(2)
            Q.ptT_ring = Ring(2)
            Q.silu_ring = Ring(2)
            Q.hT_ring = Ring(24)
            Q.outr_ring = Ring(2)
            Q.pG_ring = Ring(1)
            Q.pU_ring = Ring(1)
            Q.pB_ring = [Ring(1) for _ in range(4)]

            if debug_phase in ("p2ln", "p2a", "p2b"):
                _emit_phase2(sc, T, Q, stage_limit={"p2ln": 1, "p2a": 2, "p2b": 3}[debug_phase])
            elif debug_phase == "p1":
                sq = sc.sems
                for e in ENGINES:
                    sc.wait(e, sq["bar"], 5)
                for gt in range(TOK // 128):
                    sc.wait("sync", sq["x2o"], 16 * (gt + 1))
                    slot = gt % 5
                    c_l = sc.dma("sync", lambda e, gt=gt, slot=slot: e.dma_start(
                        out=Q.x2ring[:, 1024 * slot:1024 * (slot + 1)],
                        in_=T.x2_d[128 * gt:128 * (gt + 1), :]), sq["x2i"])
                    sc.wait("sync", sq["x2i"], c_l)
                    sc.dma("sync", lambda e, gt=gt, slot=slot: e.dma_start(
                        out=T.y_d[128 * gt:128 * (gt + 1), :],
                        in_=Q.x2ring[:, 1024 * slot:1024 * (slot + 1)]), sq["y"])
                sc.ops["sync"].append(
                    lambda e: e.wait_ge(sq["y"].h, sq["y"].n))
            else:
                _emit_phase2(sc, T, Q)

            with nc.Block() as blk2:
                for ename in ENGINES:
                    getattr(blk2, ename)(
                        lambda eng, fns=list(sc.ops[ename]): [f(eng) for f in fns])

    return nc


_NC_CACHE = None


def kernel(L0_attn_x, L0_ffn_w_gate, L0_ffn_w_up, L0_ffn_w_down):
    global _NC_CACHE
    x = np.ascontiguousarray(np.asarray(L0_attn_x, dtype=np.float32))
    wg = np.asarray(L0_ffn_w_gate, dtype=np.int32)
    wu = np.asarray(L0_ffn_w_up, dtype=np.int32)
    wd = np.asarray(L0_ffn_w_down, dtype=np.int32)
    B, S, D = x.shape
    assert (B, S, D) == (16, SEQ, DM)

    wg_l = np.ascontiguousarray(
        wg.T.reshape(DM, HC, 128).transpose(1, 0, 2))
    wu_l = np.ascontiguousarray(
        wu.T.reshape(DM, HC, 128).transpose(1, 0, 2))
    wd_l = np.ascontiguousarray(wd.T.reshape(HC, 128, DM))
    idb = np.eye(128, dtype=ml_dtypes.bfloat16)
    tri = np.triu(np.ones((128, 128), dtype=np.float32)).astype(ml_dtypes.bfloat16)

    if _NC_CACHE is None:
        _NC_CACHE = build_program()
    nc = _NC_CACHE

    in_maps = []
    for i in range(NCORES):
        xc = np.ascontiguousarray(x[2 * i:2 * i + 2].reshape(TOK, DM))
        in_maps.append({"x": xc, "wg": wg_l, "wu": wu_l, "wd": wd_l,
                        "idb": idb, "tri": tri})
    res = run_bass_kernel_spmd(nc, in_maps, core_ids=list(range(NCORES)))
    out = np.empty((B, S, D), dtype=np.float32)
    for i in range(NCORES):
        out[2 * i:2 * i + 2] = res.results[i]["y"].reshape(2, SEQ, DM)
    return out
